# version 42
# baseline (speedup 1.0000x reference)
import contextlib
import numpy as np
import concourse.bass as bass
import concourse.mybir as mybir
from concourse.bass_utils import run_bass_kernel_spmd

F32 = mybir.dt.float32
BF16 = mybir.dt.bfloat16
U8 = mybir.dt.uint8
AF = mybir.ActivationFunctionType
ALU = mybir.AluOpType

C = 2048
NCT = 16
HID = 5632
NHT = 44
NTAU = 4
NTG = 8
NBLK = 32
NPAIR = 4
LNEPS = 1e-5
ATEPS = 1e-15

def _b34_sched():
    def tgt(qi):
        if qi <= 40:
            return qi - 15
        if qi <= 51:
            return 2 * qi - 55
        return qi - 4
    sched, nd_p = [], 0
    for qi in range(64):
        sched.append(("q", qi))
        while nd_p <= tgt(qi) and nd_p < 64:
            sched.append(("nd", nd_p))
            nd_p += 1
    while nd_p < 64:
        sched.append(("nd", nd_p))
        nd_p += 1
    return sched


B34_SCHED = _b34_sched()


def build_program():
    nc = bass.Bass("TRN2", target_bir_lowering=False)

    xT = nc.dram_tensor("xT", [NCT, 128, 2048], F32, kind="ExternalInput").ap()
    wqT = nc.dram_tensor("wqT", [16, 128, 2048], F32, kind="ExternalInput").ap()
    wkvT = nc.dram_tensor("wkvT", [8, 128, 8192], F32, kind="ExternalInput").ap()
    wpT = nc.dram_tensor("wpT", [16, 128, 2048], F32, kind="ExternalInput").ap()
    wgu8 = nc.dram_tensor("wgu8", [NHT, 128, 4096], U8, kind="ExternalInput").ap()
    wd8 = nc.dram_tensor("wd8", [16, 128, HID], U8, kind="ExternalInput").ap()
    bprojT = nc.dram_tensor("bprojT", [128, 16], F32, kind="ExternalInput").ap()
    sgT = nc.dram_tensor("sgT", [128, NHT], F32, kind="ExternalInput").ap()
    suT = nc.dram_tensor("suT", [128, NHT], F32, kind="ExternalInput").ap()
    sdT = nc.dram_tensor("sdT", [128, 16], F32, kind="ExternalInput").ap()
    onesc = nc.dram_tensor("onesc", [128, 1], F32, kind="ExternalInput").ap()
    onesr = nc.dram_tensor("onesr", [1, 128], F32, kind="ExternalInput").ap()
    maskc = nc.dram_tensor("maskc", [128, 128], F32, kind="ExternalInput").ap()
    epsc = nc.dram_tensor("epsc", [128, 2], F32, kind="ExternalInput").ap()
    out_d = nc.dram_tensor("out", [16, 128, 2048], F32, kind="ExternalOutput").ap()
    cc_in_d = nc.dram_tensor("cc_in_d", [128, 2064], F32, kind="Internal").ap()
    cc_out_d = nc.dram_tensor("cc_out_d", [128, 2064], F32, kind="Internal").ap()

    with contextlib.ExitStack() as _st0:
        onecol_bf = _st0.enter_context(nc.sbuf_tensor("onecol_bf", [128, 1], BF16))
        onesrow_f = _st0.enter_context(nc.sbuf_tensor("onesrow_f", [1, 128], F32))
        mask_bf = _st0.enter_context(nc.sbuf_tensor("mask_bf", [128, 128], BF16))
        bproj_sb = _st0.enter_context(nc.sbuf_tensor("bproj_sb", [128, 16], F32))
        sg_sb = _st0.enter_context(nc.sbuf_tensor("sg_sb", [128, NHT], F32))
        su_sb = _st0.enter_context(nc.sbuf_tensor("su_sb", [128, NHT], F32))
        sd_sb = _st0.enter_context(nc.sbuf_tensor("sd_sb", [128, 16], F32))
        xn_loc = _st0.enter_context(nc.sbuf_tensor("xn_loc", [128, NCT * 2048], BF16))
        g_wq = nc.sbuf_tensor("wq_buf", [128, 2 * 2048], BF16, side="right")
        wq_buf = g_wq.__enter__()
        epsb = _st0.enter_context(nc.sbuf_tensor("epsb", [128, 2], F32))
        scst = _st0.enter_context(nc.semaphore("scst"))
        sxd = [_st0.enter_context(nc.semaphore(f"sxd{i}")) for i in range(2)]
        ssq = _st0.enter_context(nc.semaphore("ssq"))
        sstc = _st0.enter_context(nc.semaphore("sstc"))
        svA = _st0.enter_context(nc.semaphore("svA"))
        ssr = _st0.enter_context(nc.semaphore("ssr"))
        svB = _st0.enter_context(nc.semaphore("svB"))
        sbc = _st0.enter_context(nc.semaphore("sbc"))
        svCp = _st0.enter_context(nc.semaphore("svCp"))
        svC = _st0.enter_context(nc.semaphore("svC"))
        sgC = _st0.enter_context(nc.semaphore("sgC"))
        swkv = [_st0.enter_context(nc.semaphore(f"swkv{i}")) for i in range(2)]
        skvmm = _st0.enter_context(nc.semaphore("skvmm"))
        skev = _st0.enter_context(nc.semaphore("skev"))
        svev = _st0.enter_context(nc.semaphore("svev"))
        svk = _st0.enter_context(nc.semaphore("svk"))
        swq = [_st0.enter_context(nc.semaphore(f"swq{i}")) for i in range(2)]
        sqmm = _st0.enter_context(nc.semaphore("sqmm"))
        sqev = _st0.enter_context(nc.semaphore("sqev"))
        snd = _st0.enter_context(nc.semaphore("snd"))
        ssrec = _st0.enter_context(nc.semaphore("ssrec"))
        svattn = _st0.enter_context(nc.semaphore("svattn"))
        swp = [_st0.enter_context(nc.semaphore(f"swp{i}")) for i in range(2)]
        sxres = [_st0.enter_context(nc.semaphore(f"sxres{i}")) for i in range(2)]
        spmm = _st0.enter_context(nc.semaphore("spmm"))
        so1ev = _st0.enter_context(nc.semaphore("so1ev"))
        so1st = [_st0.enter_context(nc.semaphore(f"so1st{i}")) for i in range(2)]
        swgu = [_st0.enter_context(nc.semaphore(f"swgu{i}")) for i in range(4)]
        ssubg = _st0.enter_context(nc.semaphore("ssubg"))
        ssubd = _st0.enter_context(nc.semaphore("ssubd"))
        swd = [_st0.enter_context(nc.semaphore(f"swd{i}")) for i in range(2)]
        sgumm = _st0.enter_context(nc.semaphore("sgumm"))
        sgev = _st0.enter_context(nc.semaphore("sgev"))
        shid = _st0.enter_context(nc.semaphore("shid"))
        sdmm = _st0.enter_context(nc.semaphore("sdmm"))
        sfev = _st0.enter_context(nc.semaphore("sfev"))
        sfst = [_st0.enter_context(nc.semaphore(f"sfst{i}")) for i in range(2)]
        sccv = _st0.enter_context(nc.semaphore("sccv"))
        sccd = _st0.enter_context(nc.semaphore("sccd"))
        scc = _st0.enter_context(nc.semaphore("scc"))
        sccb = _st0.enter_context(nc.semaphore("sccb"))
        sSb = _st0.enter_context(nc.semaphore("sSb"))
        g_wkv = [nc.sbuf_tensor("wk_buf", [128, 2 * NCT * 512], BF16),
                 nc.sbuf_tensor("wv_buf", [128, 2 * NCT * 512], BF16)]
        wk_buf, wv_buf = [g.__enter__() for g in g_wkv]

        def wkv_issue(g, pair):
            sl = 8192 * (pair % 2)
            g.wait_ge(swkv[pair % 2], 32 * (pair // 2))
            g.dma_start(wk_buf[:, sl: sl + 8192], wkvT[pair, :, :]).then_inc(swkv[pair % 2], 16)
            g.dma_start(wv_buf[:, sl: sl + 8192], wkvT[4 + pair, :, :]).then_inc(swkv[pair % 2], 16)

        def wq_issue(g, ot):
            g.wait_ge(swq[ot % 2], 16 * (ot // 2))
            g.dma_start(wq_buf[:, 2048 * (ot % 2): 2048 * (ot % 2) + 2048],
                        wqT[ot, :, :]).then_inc(swq[ot % 2], 16)

        def xn_dst_tile(tg, ct):
            return xn_loc[:, 2048 * ct + 512 * tg: 2048 * ct + 512 * tg + 512]

        g_vk = [nc.psum_tensor(f"vk_ps{i}", [128, 512], F32) for i in range(4)]
        vk_ps = [g.__enter__() for g in g_vk]
        g_ks = nc.psum_tensor("ksum_ps", [128, 512], F32)
        ksum_ps = g_ks.__enter__()

        with contextlib.ExitStack() as _st1:
            xbf_st = _st1.enter_context(nc.sbuf_tensor("xbf_st", [128, 2 * NCT * 512], BF16))
            x2_st = _st1.enter_context(nc.sbuf_tensor("x2_st", [128, 4 * 512], BF16))
            m_row = _st1.enter_context(nc.sbuf_tensor("m_row", [1, 512], F32))
            ex2_row = _st1.enter_context(nc.sbuf_tensor("ex2_row", [1, 512], F32))
            msq_row = _st1.enter_context(nc.sbuf_tensor("msq_row", [1, 512], F32))
            var_row = _st1.enter_context(nc.sbuf_tensor("var_row", [1, 512], F32))
            lnv_row = _st1.enter_context(nc.sbuf_tensor("lnv_row", [1, 512], F32))
            r_row = _st1.enter_context(nc.sbuf_tensor("r_row", [1, 512], F32))
            s0_row = _st1.enter_context(nc.sbuf_tensor("s0_row", [1, 512], F32))
            s1b_sb = _st1.enter_context(nc.sbuf_tensor("s1b_sb", [128, 512], BF16))
            s0b_sb = _st1.enter_context(nc.sbuf_tensor("s0b_sb", [128, 512], BF16))
            tmp_v = _st1.enter_context(nc.sbuf_tensor("tmp_v", [128, 512], BF16))
            tmp_g = _st1.enter_context(nc.sbuf_tensor("tmp_g", [128, 512], BF16))
            cc_in_sb = _st1.enter_context(nc.sbuf_tensor("cc_in_sb", [128, 2064], F32))
            k_buf = _st1.enter_context(nc.sbuf_tensor("k_buf", [128, 2 * 512], BF16))
            v_buf = _st1.enter_context(nc.sbuf_tensor("v_buf", [128, 2 * 512], BF16))
            kv_rot0 = _st1.enter_context(nc.psum_tensor("kv_rot0", [128, 512], F32))
            kv_rot1 = _st1.enter_context(nc.psum_tensor("kv_rot1", [128, 512], F32))
            kv_rot2 = _st1.enter_context(nc.psum_tensor("kv_rot2", [128, 512], F32))
            block = _st1.enter_context(nc.Block("b12", no_gpsimd_drain=True))
            kv_rot = [kv_rot0, kv_rot1, kv_rot2]
            ps_sx, ps_sx2 = vk_ps[2], vk_ps[3]
            ps_b1, ps_b2 = vk_ps[2], vk_ps[3]

            def xbf(tg, ct):
                s = tg % 2
                return xbf_st[:, 8192 * s + 512 * ct: 8192 * s + 512 * ct + 512]

            def x2t(ci):
                return x2_st[:, 512 * (ci % 4): 512 * (ci % 4) + 512]

            def xn_slice(blk, ct):
                return xn_loc[:, 2048 * ct + 128 * blk: 2048 * ct + 128 * blk + 128]

            @block.gpsimd
            def _(g):
                def issue_dma(tg):
                    s = tg % 2
                    dst = xbf_st[:, 8192 * s: 8192 * s + 8192].rearrange(
                        "p (ct j) -> p ct j", ct=NCT)
                    src = xT[:, :, 512 * tg: 512 * tg + 512].rearrange(
                        "ct p j -> p ct j")
                    g.wait_ge(sxd[s], 32 * (tg // 2))
                    g.dma_start(dst[:, 0:8, :], src[:, 0:8, :]).then_inc(sxd[s], 16)
                    g.dma_start(dst[:, 8:16, :], src[:, 8:16, :]).then_inc(sxd[s], 16)

                issue_dma(0)
                g.dma_start(onecol_bf[:, :], onesc[:, :]).then_inc(scst, 16)
                g.dma_start(onesrow_f[:, :], onesr[:, :]).then_inc(scst, 16)
                g.dma_start(mask_bf[:, :], maskc[:, :]).then_inc(scst, 16)
                g.dma_start(bproj_sb[:, :], bprojT[:, :]).then_inc(scst, 16)
                g.dma_start(sg_sb[:, :], sgT[:, :]).then_inc(scst, 16)
                g.dma_start(su_sb[:, :], suT[:, :]).then_inc(scst, 16)
                g.dma_start(sd_sb[:, :], sdT[:, :]).then_inc(scst, 16)
                g.dma_start(epsb[:, :], epsc[:, :]).then_inc(scst, 16)
                issue_dma(1)
                wkv_issue(g, 0)
                for tg in range(NTAU):
                    g.wait_ge(svCp, tg + 1)
                    for ct in range(10, NCT):
                        g.tensor_tensor(tmp_g[:, :], xbf(tg, ct), s1b_sb[:, :], ALU.mult)
                        ins = g.tensor_tensor(xn_dst_tile(tg, ct), tmp_g[:, :],
                                              s0b_sb[:, :], ALU.add)
                        if ct == NCT - 1:
                            ins.then_inc(sgC, 1)
                    if tg + 2 < NTAU:
                        g.wait_ge(svC, tg + 1)
                        issue_dma(tg + 2)
                    if tg == 1:
                        wkv_issue(g, 1)
                for pair in range(2, NPAIR):
                    g.wait_ge(svk, 16 * (pair - 1))
                    wkv_issue(g, pair)
                wq_issue(g, 0)
                wq_issue(g, 1)
                g.wait_ge(sccd, 64)
                g.collective_compute(
                    "AllReduce", ALU.add,
                    replica_groups=[[0, 1], [2, 3], [4, 5], [6, 7]],
                    ins=[cc_in_d[:, :].opt()], outs=[cc_out_d[:, :].opt()],
                ).then_inc(scc, 1)

            @block.scalar
            def _(s):
                sq_next = [0]

                def emit_sq_until(limit):
                    while sq_next[0] < min(limit, NCT * NTAU):
                        ci = sq_next[0]
                        tg2, ct2 = ci // NCT, ci % NCT
                        if ct2 == 0:
                            s.wait_ge(sxd[tg2 % 2], 32 * (tg2 // 2) + 16)
                        elif ct2 == 8:
                            s.wait_ge(sxd[tg2 % 2], 32 * (tg2 // 2 + 1))
                        if ci >= 4:
                            s.wait_ge(sstc, ci - 3)
                        s.activation(x2t(ci), xbf(tg2, ct2), AF.Square).then_inc(ssq, 1)
                        sq_next[0] += 1

                def kv_relu(i):
                    s.wait_ge(skvmm, 2 * i + 1)
                    if i >= 2:
                        s.wait_ge(svk, i - 1)
                    s.activation(k_buf[:, 512 * (i % 2): 512 * (i % 2) + 512],
                                 kv_rot[(2 * i) % 3][:, :], AF.Relu).then_inc(skev, 1)

                for tg in range(NTAU):
                    emit_sq_until(NCT * (tg + 1) + (0 if tg == 1 else 4))
                    s.wait_ge(svA, 2 * tg + 2)
                    if tg >= 1:
                        s.wait_ge(sbc, tg)
                        s.wait_ge(svB, tg)
                    s.activation(lnv_row[:, :], var_row[:, :], AF.Ln, bias=epsb[0:1, 0:1])
                    s.activation(r_row[:, :], lnv_row[:, :], AF.Exp, scale=-0.5).then_inc(ssr, 1)
                    if tg == 1:
                        for i2 in range(0, 8):
                            kv_relu(i2)
                    elif tg == 2:
                        for i2 in range(8, 12):
                            kv_relu(i2)
                for i in range(12, 64):
                    kv_relu(i)

            @block.tensor
            def _(t):
                def vk_mms(i):
                    pair_i, blk_i = i // 16, i % 16
                    if i == 32:
                        t.wait_ge(svCp, 4)
                    t.wait_ge(skev, i + 1)
                    t.wait_ge(svev, i + 1)
                    for gg in range(4):
                        kb = k_buf[:, 512 * (i % 2) + 128 * gg: 512 * (i % 2) + 128 * gg + 128]
                        vb = v_buf[:, 512 * (i % 2) + 128 * gg: 512 * (i % 2) + 128 * gg + 128]
                        t.matmul(vk_ps[pair_i][:, 128 * gg: 128 * gg + 128], kb, vb,
                                 start=(blk_i == 0 and gg == 0),
                                 stop=(blk_i == 15 and gg == 3),
                                 skip_group_check=True)
                        col = 4 * pair_i + gg
                        ins = t.matmul(ksum_ps[:, col: col + 1], kb, onecol_bf[:, :],
                                       start=(i == 0 and gg == 0),
                                       stop=(i == 63 and gg == 3),
                                       skip_group_check=True)
                        if gg == 3:
                            ins.then_inc(svk, 1)

                def kv_blk(pair, blk):
                    sl = 8192 * (pair % 2)
                    i = 16 * pair + blk
                    if pair == 0 and blk % 4 == 0:
                        t.wait_ge(svC, blk // 4 + 1)
                        t.wait_ge(sgC, blk // 4 + 1)
                    if blk == 0:
                        if i >= 1:
                            vk_mms(i - 1)
                        t.wait_ge(swkv[pair % 2], 32 * (pair // 2 + 1))
                    bk = kv_rot[(2 * i) % 3]
                    bv = kv_rot[(2 * i + 1) % 3]
                    if 2 * i >= 3:
                        t.wait_ge(svev, i - 1)
                    for ct in range(NCT):
                        ins = t.matmul(bk[:, :], xn_slice(blk, ct),
                                       wk_buf[:, sl + 512 * ct: sl + 512 * ct + 512],
                                       start=(ct == 0), stop=(ct == NCT - 1))
                        if ct == NCT - 1:
                            ins.then_inc(skvmm, 1)
                    if blk >= 1:
                        vk_mms(i - 1)
                    if 2 * i + 1 >= 3:
                        t.wait_ge(skev, i)
                    for ct in range(NCT):
                        ins = t.matmul(bv[:, :], xn_slice(blk, ct),
                                       wv_buf[:, sl + 512 * ct: sl + 512 * ct + 512],
                                       start=(ct == 0), stop=(ct == NCT - 1))
                        if ct == NCT - 1:
                            ins.then_inc(skvmm, 1)

                t.wait_ge(scst, 16 * 8)
                for tg in range(NTAU):
                    t.wait_ge(sxd[tg % 2], 32 * (tg // 2) + 16)
                    if tg >= 1:
                        t.wait_ge(svCp, tg)
                    for ct in range(NCT):
                        ci = NCT * tg + ct
                        if ct == 8:
                            t.wait_ge(sxd[tg % 2], 32 * (tg // 2 + 1))
                        t.wait_ge(ssq, ci + 1)
                        t.matmul(ps_sx[0:1, :], onecol_bf[:, :], xbf(tg, ct),
                                 start=(ct == 0), stop=(ct == NCT - 1))
                        t.matmul(ps_sx2[0:1, :], onecol_bf[:, :], x2t(ci),
                                 start=(ct == 0), stop=(ct == NCT - 1)).then_inc(sstc, 1)
                    t.wait_ge(ssr, tg + 1)
                    t.wait_ge(svB, tg + 1)
                    t.wait_ge(svA, 2 * tg + 2)
                    t.matmul(ps_b1[:, :], onesrow_f[0:1, :], r_row[0:1, :],
                             start=True, stop=True)
                    t.matmul(ps_b2[:, :], onesrow_f[0:1, :], s0_row[0:1, :],
                             start=True, stop=True).then_inc(sbc, 1)
                    if tg == 1:
                        for blk2 in range(0, 8):
                            kv_blk(0, blk2)
                    elif tg == 2:
                        for blk2 in range(8, 12):
                            kv_blk(0, blk2)

                for blk in range(12, 16):
                    kv_blk(0, blk)
                for pair in range(1, NPAIR):
                    for blk in range(16):
                        kv_blk(pair, blk)
                vk_mms(63)

            @block.vector
            def _(v):
                def kv_vev(i):
                    v.wait_ge(skvmm, 2 * i + 2)
                    if i >= 2:
                        v.wait_ge(svk, i - 1)
                    v.tensor_copy(v_buf[:, 512 * (i % 2): 512 * (i % 2) + 512],
                                  kv_rot[(2 * i + 1) % 3][:, :]).then_inc(svev, 1)

                for tg in range(NTAU):
                    v.wait_ge(sstc, NCT * (tg + 1))
                    if tg >= 1:
                        v.wait_ge(ssr, tg)
                    v.tensor_scalar(m_row[:, :], ps_sx[0:1, :], 1.0 / C, None, ALU.mult)
                    v.tensor_scalar(ex2_row[:, :], ps_sx2[0:1, :], 1.0 / C, None,
                                    ALU.mult).then_inc(svA, 1)
                    v.tensor_tensor(msq_row[:, :], m_row[:, :], m_row[:, :], ALU.mult)
                    v.tensor_sub(var_row[:, :], ex2_row[:, :], msq_row[:, :]).then_inc(svA, 1)
                    v.wait_ge(ssr, tg + 1)
                    v.scalar_tensor_tensor(s0_row[:, :], m_row[:, :], -1.0, r_row[:, :],
                                           ALU.mult, ALU.mult).then_inc(svB, 1)
                    v.wait_ge(sbc, tg + 1)
                    if tg >= 1:
                        v.wait_ge(sgC, tg)
                    v.tensor_copy(s1b_sb[:, :], ps_b1[:, :])
                    v.tensor_copy(s0b_sb[:, :], ps_b2[:, :]).then_inc(svCp, 1)
                    for ct in range(10):
                        v.tensor_tensor(tmp_v[:, :], xbf(tg, ct), s1b_sb[:, :], ALU.mult)
                        ins = v.tensor_tensor(xn_dst_tile(tg, ct), tmp_v[:, :],
                                              s0b_sb[:, :], ALU.add)
                        if ct == 9:
                            ins.then_inc(svC, 1)
                    if tg == 1:
                        for i2 in range(0, 8):
                            kv_vev(i2)
                    elif tg == 2:
                        for i2 in range(8, 12):
                            kv_vev(i2)
                for i in range(12, 64):
                    v.wait_ge(skvmm, 2 * i + 2)
                    if i >= 2:
                        v.wait_ge(svk, i - 1)
                    v.tensor_copy(v_buf[:, 512 * (i % 2): 512 * (i % 2) + 512],
                                  kv_rot[(2 * i + 1) % 3][:, :]).then_inc(svev, 1)
                    if i in (18, 34, 50):
                        p2 = i // 16 - 1
                        v.wait_ge(svk, 16 * (p2 + 1))
                        v.tensor_copy(cc_in_sb[:, 512 * p2: 512 * p2 + 512],
                                      vk_ps[p2][:, :]).then_inc(sccv, 1)
                v.wait_ge(svk, 64)
                v.tensor_copy(cc_in_sb[:, 1536: 2048], vk_ps[3][:, :])
                v.tensor_copy(cc_in_sb[:, 2048: 2064], ksum_ps[:, 0:16]).then_inc(sccv, 1)

            @block.sync
            def _(sy):
                for p2 in range(4):
                    sy.wait_ge(sccv, p2 + 1)
                    lo = 512 * p2
                    hi = 2064 if p2 == 3 else 512 * p2 + 512
                    sy.dma_start(cc_in_d[:, lo: hi],
                                 cc_in_sb[:, lo: hi]).then_inc(sccd, 16)

        for g in reversed(g_wkv):
            g.__exit__(None, None, None)
        g_ks.__exit__(None, None, None)
        for g in reversed(g_vk):
            g.__exit__(None, None, None)

        g_b5pre = [nc.sbuf_tensor("wgu_buf", [128, 4 * 4096], BF16)]
        wgu_buf, = [g.__enter__() for g in g_b5pre]
        g_q = nc.sbuf_tensor("q_sb", [128, NCT * 2048], BF16)
        q_sb = g_q.__enter__()
        g_sS = [nc.sbuf_tensor("snum_sb", [128, 16 * 128], BF16),
                nc.sbuf_tensor("sden_sb", [128, 16 * 128], BF16),
                nc.sbuf_tensor("wp_buf", [128, 2 * 2048], BF16),
                nc.sbuf_tensor("xres_buf", [128, 2 * 2048], BF16),
                nc.sbuf_tensor("ks_sb", [128, 16], F32)]
        snum_sb, sden_sb, wp_buf, xres_buf, ks_sb = [g.__enter__() for g in g_sS]

        def wgu_issue(g, gi2):
            if gi2 >= 4:
                g.wait_ge(sgumm, 2 * (gi2 - 3))
            g.wait_ge(swgu[gi2 % 4], 16 * (gi2 // 4))
            g.dma_start(wgu_buf[:, 4096 * (gi2 % 4): 4096 * (gi2 % 4) + 4096],
                        wgu8[gi2 % NHT, :, :]).then_inc(swgu[gi2 % 4], 16)

        def wp_issue(g, dt):
            g.wait_ge(swp[dt % 2], 16 * (dt // 2))
            g.dma_start(wp_buf[:, 2048 * (dt % 2): 2048 * (dt % 2) + 2048],
                        wpT[dt, :, :]).then_inc(swp[dt % 2], 16)

        def xres_issue(g, dt):
            g.wait_ge(sxres[dt % 2], 16 * (dt // 2))
            g.dma_start(xres_buf[:, 2048 * (dt % 2): 2048 * (dt % 2) + 2048],
                        xT[dt, :, 0:2048]).then_inc(sxres[dt % 2], 16)

        with contextlib.ExitStack() as _st3:
            q_rot0 = _st3.enter_context(nc.psum_tensor("q_rot0", [128, 512], F32))
            q_rot1 = _st3.enter_context(nc.psum_tensor("q_rot1", [128, 512], F32))
            q_rot2 = _st3.enter_context(nc.psum_tensor("q_rot2", [128, 512], F32))
            nd_num0 = _st3.enter_context(nc.psum_tensor("nd_num0", [128, 512], F32))
            nd_num1 = _st3.enter_context(nc.psum_tensor("nd_num1", [128, 512], F32))
            nd_den0 = _st3.enter_context(nc.psum_tensor("nd_den0", [128, 512], F32))
            nd_den1 = _st3.enter_context(nc.psum_tensor("nd_den1", [128, 512], F32))
            rec_st = _st3.enter_context(nc.sbuf_tensor("rec_st", [128, 3 * 512], F32))
            out1_st = _st3.enter_context(nc.sbuf_tensor("out1_st", [128, 4 * 512], F32))
            block = _st3.enter_context(nc.Block("b34", no_gpsimd_drain=True))
            q_rot = [q_rot0, q_rot1, q_rot2]
            nd_num = [nd_num0, nd_num1]
            nd_den = [nd_den0, nd_den1]
            pj_rot = [q_rot0, q_rot1]

            @block.sync
            def _(sy):
                sy.wait_ge(scc, 1)
                sy.dma_start(ks_sb[:, :], cc_out_d[:, 2048: 2064]).then_inc(sccb, 16)
                for idx in range(64):
                    dt, tau = idx // 4, idx % 4
                    sy.wait_ge(so1ev, idx + 1)
                    sy.wait_ge(so1st[idx % 2], 16 * (idx // 2))
                    sy.dma_start(out_d[dt, :, 512 * tau: 512 * tau + 512],
                                 out1_st[:, 512 * (idx % 4): 512 * (idx % 4) + 512]
                                 ).then_inc(so1st[idx % 2], 16)


            @block.gpsimd
            def _(g):
                for ot in range(2, 16):
                    g.wait_ge(sqmm, 4 * (ot - 1))
                    wq_issue(g, ot)
                    if ot == 2:
                        g.wait_ge(scc, 1)
                        g.dma_start(snum_sb[:, :], cc_out_d[:, 0: 2048]).then_inc(sccb, 16)
                wp_issue(g, 0)
                xres_issue(g, 0)
                wp_issue(g, 1)
                xres_issue(g, 1)
                for dt in range(2, 16):
                    g.wait_ge(spmm, 4 * (dt - 1))
                    wp_issue(g, dt)
                    g.wait_ge(so1ev, 4 * (dt - 1))
                    xres_issue(g, dt)
                for gi2 in range(4):
                    wgu_issue(g, gi2)

            @block.tensor
            def _(t):
                def nd_pair(j):
                    gq, tau = j // 4, j % 4
                    qs = q_sb[:, 2048 * gq + 512 * tau: 2048 * gq + 512 * tau + 512]
                    t.wait_ge(sqev, j + 1)
                    t.wait_ge(sSb, gq + 1)
                    if j >= 2:
                        t.wait_ge(ssrec, 2 * j - 3)
                    t.matmul(nd_den[j % 2][:, :], sden_sb[:, 128 * gq: 128 * gq + 128],
                             qs, start=True, stop=True).then_inc(snd, 1)
                    if j >= 2:
                        t.wait_ge(svattn, j - 1)
                    t.matmul(nd_num[j % 2][:, :], snum_sb[:, 128 * gq: 128 * gq + 128],
                             qs, start=True, stop=True).then_inc(snd, 1)

                for kind, i in B34_SCHED:
                    if kind == "nd":
                        nd_pair(i)
                        continue
                    ot, tau = i // 4, i % 4
                    if tau == 0:
                        t.wait_ge(swq[ot % 2], 16 * (ot // 2 + 1))
                    bank = q_rot[i % 3]
                    if i >= 3:
                        t.wait_ge(sqev, i - 2)
                    for ct in range(NCT):
                        ins = t.matmul(
                            bank[:, :],
                            wq_buf[:, 2048 * (ot % 2) + 128 * ct: 2048 * (ot % 2) + 128 * ct + 128],
                            xn_loc[:, 2048 * ct + 512 * tau: 2048 * ct + 512 * tau + 512],
                            start=(ct == 0), stop=(ct == NCT - 1))
                        if ct == NCT - 1:
                            ins.then_inc(sqmm, 1)
                t.wait_ge(sqev, 64)
                for dt in range(16):
                    t.wait_ge(swp[dt % 2], 16 * (dt // 2 + 1))
                    for tau in range(NTAU):
                        idx = 4 * dt + tau
                        t.wait_ge(svattn, 61 + tau)
                        if idx >= 2:
                            t.wait_ge(so1ev, idx - 1)
                        bank = pj_rot[idx % 2]
                        for ct in range(NCT):
                            ins = t.matmul(
                                bank[:, :],
                                wp_buf[:, 2048 * (dt % 2) + 128 * ct: 2048 * (dt % 2) + 128 * ct + 128],
                                q_sb[:, 2048 * ct + 512 * tau: 2048 * ct + 512 * tau + 512],
                                start=(ct == 0), stop=(ct == NCT - 1))
                            if ct == NCT - 1:
                                ins.then_inc(spmm, 1)

            @block.scalar
            def _(s):
                def relu(k):
                    ot, tau = k // 4, k % 4
                    s.wait_ge(sqmm, k + 1)
                    s.activation(q_sb[:, 2048 * ot + 512 * tau: 2048 * ot + 512 * tau + 512],
                                 q_rot[k % 3][:, :], AF.Relu).then_inc(sqev, 1)

                def rec(j):
                    sl = rec_st[:, 512 * (j % 3): 512 * (j % 3) + 512]
                    s.wait_ge(snd, 2 * j + 1)
                    if j >= 3:
                        s.wait_ge(svattn, j - 2)
                    s.activation(sl, nd_den[j % 2][:, :], AF.Ln,
                                 bias=epsb[:, 1:2]).then_inc(ssrec, 1)
                    s.activation(sl, sl, AF.Exp, scale=-1.0).then_inc(ssrec, 1)

                for kind, i in B34_SCHED:
                    if kind == "q":
                        relu(i)
                    else:
                        rec(i)

            @block.vector
            def _(v):
                v.wait_ge(sccb, 32)
                for gq in range(16):
                    v.tensor_tensor(snum_sb[:, 128 * gq: 128 * gq + 128],
                                    snum_sb[:, 128 * gq: 128 * gq + 128],
                                    mask_bf[:, :], ALU.mult)
                    v.tensor_scalar(sden_sb[:, 128 * gq: 128 * gq + 128],
                                    mask_bf[:, :], ks_sb[:, gq: gq + 1],
                                    None, ALU.mult).then_inc(sSb, 1)
                for j in range(64):
                    gq, tau = j // 4, j % 4
                    v.wait_ge(snd, 2 * j + 2)
                    v.wait_ge(ssrec, 2 * j + 2)
                    v.tensor_tensor(
                        q_sb[:, 2048 * gq + 512 * tau: 2048 * gq + 512 * tau + 512],
                        nd_num[j % 2][:, :],
                        rec_st[:, 512 * (j % 3): 512 * (j % 3) + 512],
                        ALU.mult).then_inc(svattn, 1)
                for idx in range(64):
                    dt, tau = idx // 4, idx % 4
                    v.wait_ge(spmm, idx + 1)
                    v.wait_ge(sxres[dt % 2], 16 * (dt // 2 + 1))
                    if idx >= 4:
                        v.wait_ge(so1st[idx % 2], 16 * (idx // 2 - 1))
                    v.scalar_tensor_tensor(
                        out1_st[:, 512 * (idx % 4): 512 * (idx % 4) + 512],
                        pj_rot[idx % 2][:, :],
                        bproj_sb[:, dt: dt + 1],
                        xres_buf[:, 2048 * (dt % 2) + 512 * tau: 2048 * (dt % 2) + 512 * tau + 512],
                        ALU.add, ALU.add).then_inc(so1ev, 1)

        g_wq.__exit__(None, None, None)
        for g in reversed(g_sS):
            g.__exit__(None, None, None)
        g_q.__exit__(None, None, None)

        with contextlib.ExitStack() as _st5:
            hid_sb = _st5.enter_context(nc.sbuf_tensor("hid_sb", [128, (NHT + 3) * 512], BF16))

            def hsl(tau, ht):
                return 512 * ((NHT * tau + ht) % (NHT + 3))
            gt_st = _st5.enter_context(nc.sbuf_tensor("gt_st", [128, 3 * 512], BF16))
            wd_buf = _st5.enter_context(nc.sbuf_tensor("wd_buf", [128, 2 * HID], BF16))
            outf_st = _st5.enter_context(nc.sbuf_tensor("outf_st", [128, 2 * 512], F32))
            g_rot0 = _st5.enter_context(nc.psum_tensor("g_rot0", [128, 512], F32))
            g_rot1 = _st5.enter_context(nc.psum_tensor("g_rot1", [128, 512], F32))
            g_rot2 = _st5.enter_context(nc.psum_tensor("g_rot2", [128, 512], F32))
            u_rot0 = _st5.enter_context(nc.psum_tensor("u_rot0", [128, 512], F32))
            u_rot1 = _st5.enter_context(nc.psum_tensor("u_rot1", [128, 512], F32))
            u_rot2 = _st5.enter_context(nc.psum_tensor("u_rot2", [128, 512], F32))
            d_rot0 = _st5.enter_context(nc.psum_tensor("d_rot0", [128, 512], F32))
            d_rot1 = _st5.enter_context(nc.psum_tensor("d_rot1", [128, 512], F32))
            block = _st5.enter_context(nc.Block("b5", no_gpsimd_drain=True))
            g_rot = [g_rot0, g_rot1, g_rot2]
            u_rot = [u_rot0, u_rot1, u_rot2]
            d_rot = [d_rot0, d_rot1]

            @block.gpsimd
            def _(g):
                def wgu_dma(gi2):
                    wgu_issue(g, gi2)

                def wd_dma(di2):
                    if di2 >= 2:
                        g.wait_ge(sdmm, di2 - 1)
                    g.wait_ge(swd[di2 % 2], 16 * (di2 // 2))
                    g.dma_start(wd_buf[:, HID * (di2 % 2): HID * (di2 % 2) + HID],
                                wd8[di2 % 16, :, :]).then_inc(swd[di2 % 2], 16)

                for tau in range(NTAU):
                    for ht in range(NHT):
                        gi = NHT * tau + ht
                        if 4 <= gi + 2 < NHT * NTAU:
                            wgu_dma(gi + 2)
                    wd_dma(16 * tau)
                    wd_dma(16 * tau + 1)
                    for dt in range(16):
                        di = 16 * tau + dt
                        if dt + 2 < 16:
                            wd_dma(di + 2)
                        if di == 0:
                            g.wait_ge(so1st[0], 16 * 32)
                            g.wait_ge(so1st[1], 16 * 32)
                        g.wait_ge(sfev, di + 1)
                        g.wait_ge(sfst[di % 2], 16 * (di // 2))
                        g.dma_start(out_d[dt, :, 512 * tau: 512 * tau + 512],
                                    outf_st[:, 512 * (di % 2): 512 * (di % 2) + 512],
                                    accum_op=ALU.add).then_inc(sfst[di % 2], 16)

            @block.tensor
            def _(t):
                for tau in range(NTAU):
                    for ht in range(NHT):
                        gi = NHT * tau + ht
                        t.wait_ge(ssubg, gi + 1)
                        gb, ub = g_rot[gi % 3], u_rot[gi % 3]
                        if gi >= 3:
                            t.wait_ge(sgev, gi - 2)
                        for ct in range(NCT):
                            ins = t.matmul(
                                gb[:, :],
                                wgu_buf[:, 4096 * (gi % 4) + 128 * ct: 4096 * (gi % 4) + 128 * ct + 128],
                                xn_loc[:, 2048 * ct + 512 * tau: 2048 * ct + 512 * tau + 512],
                                start=(ct == 0), stop=(ct == NCT - 1))
                            if ct == NCT - 1:
                                ins.then_inc(sgumm, 1)
                        if gi >= 3:
                            t.wait_ge(shid, gi - 2)
                        for ct in range(NCT):
                            ins = t.matmul(
                                ub[:, :],
                                wgu_buf[:, 4096 * (gi % 4) + 2048 + 128 * ct: 4096 * (gi % 4) + 2048 + 128 * ct + 128],
                                xn_loc[:, 2048 * ct + 512 * tau: 2048 * ct + 512 * tau + 512],
                                start=(ct == 0), stop=(ct == NCT - 1))
                            if ct == NCT - 1:
                                ins.then_inc(sgumm, 1)
                    for dt in range(16):
                        di = 16 * tau + dt
                        t.wait_ge(ssubd, di + 1)
                        t.wait_ge(shid, NHT * (tau + 1) - (1 if dt == 0 else 0))
                        db = d_rot[di % 2]
                        if di >= 2:
                            t.wait_ge(sfev, di - 1)
                        for ht in range(NHT):
                            if dt == 0 and ht == NHT - 1:
                                t.wait_ge(shid, NHT * (tau + 1))
                            ins = t.matmul(
                                db[:, :],
                                wd_buf[:, HID * (di % 2) + 128 * ht: HID * (di % 2) + 128 * ht + 128],
                                hid_sb[:, hsl(tau, ht): hsl(tau, ht) + 512],
                                start=(ht == 0), stop=(ht == NHT - 1))
                            if ht == NHT - 1:
                                ins.then_inc(sdmm, 1)

            @block.scalar
            def _(s):
                for tau in range(NTAU):
                    for ht in range(NHT):
                        gi = NHT * tau + ht
                        s.wait_ge(sgumm, 2 * gi + 1)
                        if gi >= 3:
                            s.wait_ge(shid, gi - 2)
                        s.activation(gt_st[:, 512 * (gi % 3): 512 * (gi % 3) + 512],
                                     g_rot[gi % 3][:, :], AF.Silu,
                                     scale=sg_sb[:, ht: ht + 1]).then_inc(sgev, 1)

            @block.vector
            def _(v):
                def sub_g(gi2):
                    v.wait_ge(swgu[gi2 % 4], 16 * (gi2 // 4 + 1))
                    v.tensor_scalar(wgu_buf[:, 4096 * (gi2 % 4): 4096 * (gi2 % 4) + 4096],
                                    wgu_buf[:, 4096 * (gi2 % 4): 4096 * (gi2 % 4) + 4096],
                                    -128.0, None, ALU.add).then_inc(ssubg, 1)

                def sub_d(di2):
                    v.wait_ge(swd[di2 % 2], 16 * (di2 // 2 + 1))
                    v.tensor_scalar(wd_buf[:, HID * (di2 % 2): HID * (di2 % 2) + HID],
                                    wd_buf[:, HID * (di2 % 2): HID * (di2 % 2) + HID],
                                    -128.0, None, ALU.add).then_inc(ssubd, 1)

                sub_g(0)
                sub_g(1)
                for tau in range(NTAU):
                    for ht in range(NHT):
                        gi = NHT * tau + ht
                        if 1 <= ht <= NHT - 2:
                            sub_g(gi + 1)
                        v.wait_ge(sgumm, 2 * gi + 2)
                        v.wait_ge(sgev, gi + 1)
                        if ht == 3 and tau >= 1:
                            v.wait_ge(sdmm, 16 * tau)
                        v.scalar_tensor_tensor(hid_sb[:, hsl(tau, ht): hsl(tau, ht) + 512],
                                               u_rot[gi % 3][:, :], su_sb[:, ht: ht + 1],
                                               gt_st[:, 512 * (gi % 3): 512 * (gi % 3) + 512],
                                               ALU.mult, ALU.mult).then_inc(shid, 1)
                    sub_d(16 * tau)
                    for dt in range(16):
                        di = 16 * tau + dt
                        if dt < 15:
                            sub_d(di + 1)
                        if tau + 1 < NTAU and dt in (1, 2):
                            sub_g(NHT * (tau + 1) + dt - 1)
                        v.wait_ge(sdmm, di + 1)
                        if di >= 2:
                            v.wait_ge(sfst[di % 2], 16 * (di // 2))
                        v.tensor_scalar(outf_st[:, 512 * (di % 2): 512 * (di % 2) + 512],
                                        d_rot[di % 2][:, :], sd_sb[:, dt: dt + 1],
                                        None, ALU.mult).then_inc(sfev, 1)

            @block.sync
            def _(sy):
                sy.wait_ge(sfst[0], 16 * 32)
                sy.wait_ge(sfst[1], 16 * 32)

        for g in reversed(g_b5pre):
            g.__exit__(None, None, None)

    return nc


_PROG_CACHE = {}


def _get_program():
    if "nc" not in _PROG_CACHE:
        _PROG_CACHE["nc"] = build_program()
    return _PROG_CACHE["nc"]


def _u8(a_int32):
    a = np.ascontiguousarray(a_int32)
    return a.view(np.uint8).reshape(a.shape + (4,))[..., 0]


def kernel(L0_attn_x, L0_attn_w_qkv, L0_attn_w_proj, L0_attn_b_proj,
           L0_ffn_wq_gate, L0_ffn_s_gate, L0_ffn_wq_up, L0_ffn_s_up,
           L0_ffn_wq_down, L0_ffn_s_down):
    x = np.asarray(L0_attn_x, dtype=np.float32)
    w_qkv = np.asarray(L0_attn_w_qkv, dtype=np.float32)
    w_proj = np.asarray(L0_attn_w_proj, dtype=np.float32)
    b_proj = np.asarray(L0_attn_b_proj, dtype=np.float32)
    wq_gate = _u8(np.asarray(L0_ffn_wq_gate))
    s_gate = np.asarray(L0_ffn_s_gate, dtype=np.float32)
    wq_up = _u8(np.asarray(L0_ffn_wq_up))
    s_up = np.asarray(L0_ffn_s_up, dtype=np.float32)
    wq_down = _u8(np.asarray(L0_ffn_wq_down))
    s_down = np.asarray(L0_ffn_s_down, dtype=np.float32)

    B, N, Cm = x.shape
    assert (B, N, Cm) == (4, 4096, 2048)

    def tile_oT(w, n_out_t, n_in_t, fo):
        a = np.ascontiguousarray(w.T)
        a = a.reshape(n_in_t, 128, n_out_t, fo).transpose(2, 1, 0, 3)
        return np.ascontiguousarray(a.reshape(n_out_t, 128, n_in_t * fo))

    wqTt = tile_oT(w_qkv[0:2048], 16, 16, 128).astype(np.float32)
    wkvTt = tile_oT(w_qkv[2048:6144], 8, 16, 512).astype(np.float32)
    wpTt = tile_oT(w_proj, 16, 16, 128).astype(np.float32)
    wgt = tile_oT(wq_gate, 44, 16, 128)
    wut = tile_oT(wq_up, 44, 16, 128)
    wgu8 = np.ascontiguousarray(np.concatenate([wgt, wut], axis=2))
    wd8 = np.ascontiguousarray(tile_oT(wq_down, 16, 44, 128))

    shared = {
        "wqT": wqTt, "wkvT": wkvTt, "wpT": wpTt, "wgu8": wgu8, "wd8": wd8,
        "bprojT": np.ascontiguousarray(b_proj.reshape(16, 128).T),
        "sgT": np.ascontiguousarray(s_gate.reshape(44, 128).T),
        "suT": np.ascontiguousarray(s_up.reshape(44, 128).T),
        "sdT": np.ascontiguousarray(s_down.reshape(16, 128).T),
        "onesc": np.ones((128, 1), dtype=np.float32),
        "onesr": np.ones((1, 128), dtype=np.float32),
        "maskc": np.kron(np.eye(4, dtype=np.float32),
                         np.ones((32, 32), dtype=np.float32)),
        "epsc": np.ascontiguousarray(
            np.broadcast_to(np.array([1e-5, 1e-15], dtype=np.float32), (128, 2))),
    }

    in_maps = []
    for core in range(8):
        b, h = core // 2, core % 2
        T = x[b].T
        xTc = np.ascontiguousarray(
            T[:, 2048 * h: 2048 * h + 2048].reshape(16, 128, 2048))
        m = {"xT": xTc}
        m.update(shared)
        in_maps.append(m)

    nc = _get_program()
    res = run_bass_kernel_spmd(nc, in_maps, core_ids=list(range(8)))

    out = np.empty((4, 4096, 2048), dtype=np.float32)
    for core in range(8):
        b, h = core // 2, core % 2
        oc = np.asarray(res.results[core]["out"])
        out[b, 2048 * h: 2048 * h + 2048, :] = oc.transpose(2, 0, 1).reshape(2048, 2048)
    return out


# version 43 (final)
# speedup vs baseline: 1.0132x; 1.0132x over previous
import contextlib
import numpy as np
import concourse.bass as bass
import concourse.mybir as mybir
from concourse.bass_utils import run_bass_kernel_spmd

F32 = mybir.dt.float32
BF16 = mybir.dt.bfloat16
U8 = mybir.dt.uint8
AF = mybir.ActivationFunctionType
ALU = mybir.AluOpType

C = 2048
NCT = 16
HID = 5632
NHT = 44
NTAU = 4
NTG = 8
NBLK = 32
NPAIR = 4
LNEPS = 1e-5
ATEPS = 1e-15

def _b34_sched():
    def tgt(qi):
        if qi <= 40:
            return qi - 15
        if qi <= 51:
            return 2 * qi - 55
        return qi - 4
    sched, nd_p = [], 0
    for qi in range(64):
        sched.append(("q", qi))
        while nd_p <= tgt(qi) and nd_p < 64:
            sched.append(("nd", nd_p))
            nd_p += 1
    while nd_p < 64:
        sched.append(("nd", nd_p))
        nd_p += 1
    return sched


B34_SCHED = _b34_sched()


def build_program():
    nc = bass.Bass("TRN2", target_bir_lowering=False)

    xT = nc.dram_tensor("xT", [NCT, 128, 2048], F32, kind="ExternalInput").ap()
    wqT = nc.dram_tensor("wqT", [16, 128, 2048], F32, kind="ExternalInput").ap()
    wkvT = nc.dram_tensor("wkvT", [8, 128, 8192], F32, kind="ExternalInput").ap()
    wpT = nc.dram_tensor("wpT", [16, 128, 2048], F32, kind="ExternalInput").ap()
    wgu8 = nc.dram_tensor("wgu8", [NHT, 128, 4096], U8, kind="ExternalInput").ap()
    wd8 = nc.dram_tensor("wd8", [16, 128, HID], U8, kind="ExternalInput").ap()
    bprojT = nc.dram_tensor("bprojT", [128, 16], F32, kind="ExternalInput").ap()
    sgT = nc.dram_tensor("sgT", [128, NHT], F32, kind="ExternalInput").ap()
    suT = nc.dram_tensor("suT", [128, NHT], F32, kind="ExternalInput").ap()
    sdT = nc.dram_tensor("sdT", [128, 16], F32, kind="ExternalInput").ap()
    onesc = nc.dram_tensor("onesc", [128, 1], F32, kind="ExternalInput").ap()
    onesr = nc.dram_tensor("onesr", [1, 128], F32, kind="ExternalInput").ap()
    maskc = nc.dram_tensor("maskc", [128, 128], F32, kind="ExternalInput").ap()
    epsc = nc.dram_tensor("epsc", [128, 2], F32, kind="ExternalInput").ap()
    out_d = nc.dram_tensor("out", [16, 128, 2048], F32, kind="ExternalOutput").ap()
    cc_in_d = nc.dram_tensor("cc_in_d", [128, 2064], F32, kind="Internal").ap()
    cc_out_d = nc.dram_tensor("cc_out_d", [128, 2064], F32, kind="Internal").ap()

    with contextlib.ExitStack() as _st0:
        onecol_bf = _st0.enter_context(nc.sbuf_tensor("onecol_bf", [128, 1], BF16))
        onesrow_f = _st0.enter_context(nc.sbuf_tensor("onesrow_f", [1, 128], F32))
        mask_bf = _st0.enter_context(nc.sbuf_tensor("mask_bf", [128, 128], BF16))
        bproj_sb = _st0.enter_context(nc.sbuf_tensor("bproj_sb", [128, 16], F32))
        sg_sb = _st0.enter_context(nc.sbuf_tensor("sg_sb", [128, NHT], F32))
        su_sb = _st0.enter_context(nc.sbuf_tensor("su_sb", [128, NHT], F32))
        sd_sb = _st0.enter_context(nc.sbuf_tensor("sd_sb", [128, 16], F32))
        xn_loc = _st0.enter_context(nc.sbuf_tensor("xn_loc", [128, NCT * 2048], BF16))
        g_wq = nc.sbuf_tensor("wq_buf", [128, 2 * 2048], BF16, side="right")
        wq_buf = g_wq.__enter__()
        epsb = _st0.enter_context(nc.sbuf_tensor("epsb", [128, 2], F32))
        scst = _st0.enter_context(nc.semaphore("scst"))
        sxd = [_st0.enter_context(nc.semaphore(f"sxd{i}")) for i in range(2)]
        ssq = _st0.enter_context(nc.semaphore("ssq"))
        sstc = _st0.enter_context(nc.semaphore("sstc"))
        svA = _st0.enter_context(nc.semaphore("svA"))
        ssr = _st0.enter_context(nc.semaphore("ssr"))
        svB = _st0.enter_context(nc.semaphore("svB"))
        sbc = _st0.enter_context(nc.semaphore("sbc"))
        svCp = _st0.enter_context(nc.semaphore("svCp"))
        svC = _st0.enter_context(nc.semaphore("svC"))
        sgC = _st0.enter_context(nc.semaphore("sgC"))
        swkv = [_st0.enter_context(nc.semaphore(f"swkv{i}")) for i in range(2)]
        skvmm = _st0.enter_context(nc.semaphore("skvmm"))
        skev = _st0.enter_context(nc.semaphore("skev"))
        svev = _st0.enter_context(nc.semaphore("svev"))
        svk = _st0.enter_context(nc.semaphore("svk"))
        swq = [_st0.enter_context(nc.semaphore(f"swq{i}")) for i in range(2)]
        sqmm = _st0.enter_context(nc.semaphore("sqmm"))
        sqev = _st0.enter_context(nc.semaphore("sqev"))
        snd = _st0.enter_context(nc.semaphore("snd"))
        ssrec = _st0.enter_context(nc.semaphore("ssrec"))
        svattn = _st0.enter_context(nc.semaphore("svattn"))
        swp = [_st0.enter_context(nc.semaphore(f"swp{i}")) for i in range(2)]
        sxres = [_st0.enter_context(nc.semaphore(f"sxres{i}")) for i in range(2)]
        spmm = _st0.enter_context(nc.semaphore("spmm"))
        so1ev = _st0.enter_context(nc.semaphore("so1ev"))
        so1st = [_st0.enter_context(nc.semaphore(f"so1st{i}")) for i in range(2)]
        swgu = [_st0.enter_context(nc.semaphore(f"swgu{i}")) for i in range(4)]
        ssubg = _st0.enter_context(nc.semaphore("ssubg"))
        ssubd = _st0.enter_context(nc.semaphore("ssubd"))
        swd = [_st0.enter_context(nc.semaphore(f"swd{i}")) for i in range(2)]
        sgumm = _st0.enter_context(nc.semaphore("sgumm"))
        sgev = _st0.enter_context(nc.semaphore("sgev"))
        shid = _st0.enter_context(nc.semaphore("shid"))
        sdmm = _st0.enter_context(nc.semaphore("sdmm"))
        sfev = _st0.enter_context(nc.semaphore("sfev"))
        sfst = [_st0.enter_context(nc.semaphore(f"sfst{i}")) for i in range(2)]
        sccv = _st0.enter_context(nc.semaphore("sccv"))
        sccd = _st0.enter_context(nc.semaphore("sccd"))
        scc = _st0.enter_context(nc.semaphore("scc"))
        sccb = _st0.enter_context(nc.semaphore("sccb"))
        sSb = _st0.enter_context(nc.semaphore("sSb"))
        g_wkv = [nc.sbuf_tensor("wk_buf", [128, 2 * NCT * 512], BF16),
                 nc.sbuf_tensor("wv_buf", [128, 2 * NCT * 512], BF16)]
        wk_buf, wv_buf = [g.__enter__() for g in g_wkv]

        def wkv_issue(g, pair):
            sl = 8192 * (pair % 2)
            g.wait_ge(swkv[pair % 2], 32 * (pair // 2))
            g.dma_start(wk_buf[:, sl: sl + 8192], wkvT[pair, :, :]).then_inc(swkv[pair % 2], 16)
            g.dma_start(wv_buf[:, sl: sl + 8192], wkvT[4 + pair, :, :]).then_inc(swkv[pair % 2], 16)

        def wq_issue(g, ot):
            g.wait_ge(swq[ot % 2], 16 * (ot // 2))
            g.dma_start(wq_buf[:, 2048 * (ot % 2): 2048 * (ot % 2) + 2048],
                        wqT[ot, :, :]).then_inc(swq[ot % 2], 16)

        def xn_dst_tile(tg, ct):
            return xn_loc[:, 2048 * ct + 512 * tg: 2048 * ct + 512 * tg + 512]

        g_vk = [nc.psum_tensor(f"vk_ps{i}", [128, 512], F32) for i in range(4)]
        vk_ps = [g.__enter__() for g in g_vk]
        g_ks = nc.psum_tensor("ksum_ps", [128, 512], F32)
        ksum_ps = g_ks.__enter__()

        with contextlib.ExitStack() as _st1:
            xbf_st = _st1.enter_context(nc.sbuf_tensor("xbf_st", [128, 2 * NCT * 512], BF16))
            x2_st = _st1.enter_context(nc.sbuf_tensor("x2_st", [128, 4 * 512], BF16))
            m_row = _st1.enter_context(nc.sbuf_tensor("m_row", [1, 512], F32))
            ex2_row = _st1.enter_context(nc.sbuf_tensor("ex2_row", [1, 512], F32))
            msq_row = _st1.enter_context(nc.sbuf_tensor("msq_row", [1, 512], F32))
            var_row = _st1.enter_context(nc.sbuf_tensor("var_row", [1, 512], F32))
            lnv_row = _st1.enter_context(nc.sbuf_tensor("lnv_row", [1, 512], F32))
            r_row = _st1.enter_context(nc.sbuf_tensor("r_row", [1, 512], F32))
            s0_row = _st1.enter_context(nc.sbuf_tensor("s0_row", [1, 512], F32))
            s1b_sb = _st1.enter_context(nc.sbuf_tensor("s1b_sb", [128, 512], BF16))
            s0b_sb = _st1.enter_context(nc.sbuf_tensor("s0b_sb", [128, 512], BF16))
            tmp_v = _st1.enter_context(nc.sbuf_tensor("tmp_v", [128, 512], BF16))
            tmp_g = _st1.enter_context(nc.sbuf_tensor("tmp_g", [128, 512], BF16))
            cc_in_sb = _st1.enter_context(nc.sbuf_tensor("cc_in_sb", [128, 2064], F32))
            k_buf = _st1.enter_context(nc.sbuf_tensor("k_buf", [128, 2 * 512], BF16))
            v_buf = _st1.enter_context(nc.sbuf_tensor("v_buf", [128, 2 * 512], BF16))
            kv_rot0 = _st1.enter_context(nc.psum_tensor("kv_rot0", [128, 512], F32))
            kv_rot1 = _st1.enter_context(nc.psum_tensor("kv_rot1", [128, 512], F32))
            kv_rot2 = _st1.enter_context(nc.psum_tensor("kv_rot2", [128, 512], F32))
            block = _st1.enter_context(nc.Block("b12", no_gpsimd_drain=True))
            kv_rot = [kv_rot0, kv_rot1, kv_rot2]
            ps_sx, ps_sx2 = vk_ps[2], vk_ps[3]
            ps_b1, ps_b2 = vk_ps[2], vk_ps[3]

            def xbf(tg, ct):
                s = tg % 2
                return xbf_st[:, 8192 * s + 512 * ct: 8192 * s + 512 * ct + 512]

            def x2t(ci):
                return x2_st[:, 512 * (ci % 4): 512 * (ci % 4) + 512]

            def xn_slice(blk, ct):
                return xn_loc[:, 2048 * ct + 128 * blk: 2048 * ct + 128 * blk + 128]

            @block.gpsimd
            def _(g):
                def issue_dma(tg):
                    s = tg % 2
                    dst = xbf_st[:, 8192 * s: 8192 * s + 8192].rearrange(
                        "p (ct j) -> p ct j", ct=NCT)
                    src = xT[:, :, 512 * tg: 512 * tg + 512].rearrange(
                        "ct p j -> p ct j")
                    g.wait_ge(sxd[s], 32 * (tg // 2))
                    g.dma_start(dst[:, 0:8, :], src[:, 0:8, :]).then_inc(sxd[s], 16)
                    g.dma_start(dst[:, 8:16, :], src[:, 8:16, :]).then_inc(sxd[s], 16)

                issue_dma(0)
                g.dma_start(onecol_bf[:, :], onesc[:, :]).then_inc(scst, 16)
                g.dma_start(onesrow_f[:, :], onesr[:, :]).then_inc(scst, 16)
                g.dma_start(mask_bf[:, :], maskc[:, :]).then_inc(scst, 16)
                g.dma_start(bproj_sb[:, :], bprojT[:, :]).then_inc(scst, 16)
                g.dma_start(sg_sb[:, :], sgT[:, :]).then_inc(scst, 16)
                g.dma_start(su_sb[:, :], suT[:, :]).then_inc(scst, 16)
                g.dma_start(sd_sb[:, :], sdT[:, :]).then_inc(scst, 16)
                g.dma_start(epsb[:, :], epsc[:, :]).then_inc(scst, 16)
                issue_dma(1)
                wkv_issue(g, 0)
                for tg in range(NTAU):
                    g.wait_ge(svCp, tg + 1)
                    for ct in range(10, NCT):
                        g.tensor_tensor(tmp_g[:, :], xbf(tg, ct), s1b_sb[:, :], ALU.mult)
                        ins = g.tensor_tensor(xn_dst_tile(tg, ct), tmp_g[:, :],
                                              s0b_sb[:, :], ALU.add)
                        if ct == NCT - 1:
                            ins.then_inc(sgC, 1)
                    if tg + 2 < NTAU:
                        g.wait_ge(svC, tg + 1)
                        issue_dma(tg + 2)
                    if tg == 1:
                        wkv_issue(g, 1)
                for pair in range(2, NPAIR):
                    g.wait_ge(svk, 16 * (pair - 1))
                    wkv_issue(g, pair)
                wq_issue(g, 0)
                wq_issue(g, 1)
                g.wait_ge(sccd, 64)
                g.collective_compute(
                    "AllReduce", ALU.add,
                    replica_groups=[[0, 1], [2, 3], [4, 5], [6, 7]],
                    ins=[cc_in_d[:, :].opt()], outs=[cc_out_d[:, :].opt()],
                ).then_inc(scc, 1)

            @block.scalar
            def _(s):
                sq_next = [0]

                def emit_sq_until(limit):
                    while sq_next[0] < min(limit, NCT * NTAU):
                        ci = sq_next[0]
                        tg2, ct2 = ci // NCT, ci % NCT
                        if ct2 == 0:
                            s.wait_ge(sxd[tg2 % 2], 32 * (tg2 // 2) + 16)
                        elif ct2 == 8:
                            s.wait_ge(sxd[tg2 % 2], 32 * (tg2 // 2 + 1))
                        if ci >= 4:
                            s.wait_ge(sstc, ci - 3)
                        s.activation(x2t(ci), xbf(tg2, ct2), AF.Square).then_inc(ssq, 1)
                        sq_next[0] += 1

                def kv_relu(i):
                    s.wait_ge(skvmm, 2 * i + 1)
                    if i >= 2:
                        s.wait_ge(svk, i - 1)
                    s.activation(k_buf[:, 512 * (i % 2): 512 * (i % 2) + 512],
                                 kv_rot[(2 * i) % 3][:, :], AF.Relu).then_inc(skev, 1)

                for tg in range(NTAU):
                    emit_sq_until(NCT * (tg + 1) + (0 if tg == 1 else 4))
                    s.wait_ge(svA, 2 * tg + 2)
                    if tg >= 1:
                        s.wait_ge(sbc, tg)
                        s.wait_ge(svB, tg)
                    s.activation(lnv_row[:, :], var_row[:, :], AF.Ln, bias=epsb[0:1, 0:1])
                    s.activation(r_row[:, :], lnv_row[:, :], AF.Exp, scale=-0.5).then_inc(ssr, 1)
                    if tg == 1:
                        for i2 in range(0, 8):
                            kv_relu(i2)
                    elif tg == 2:
                        for i2 in range(8, 12):
                            kv_relu(i2)
                for i in range(12, 64):
                    kv_relu(i)

            @block.tensor
            def _(t):
                def vk_mms(i):
                    pair_i, blk_i = i // 16, i % 16
                    if i == 32:
                        t.wait_ge(svCp, 4)
                    t.wait_ge(skev, i + 1)
                    t.wait_ge(svev, i + 1)
                    for gg in range(4):
                        kb = k_buf[:, 512 * (i % 2) + 128 * gg: 512 * (i % 2) + 128 * gg + 128]
                        vb = v_buf[:, 512 * (i % 2) + 128 * gg: 512 * (i % 2) + 128 * gg + 128]
                        t.matmul(vk_ps[pair_i][:, 128 * gg: 128 * gg + 128], kb, vb,
                                 start=(blk_i == 0 and gg == 0),
                                 stop=(blk_i == 15 and gg == 3),
                                 skip_group_check=True)
                        col = 4 * pair_i + gg
                        ins = t.matmul(ksum_ps[:, col: col + 1], kb, onecol_bf[:, :],
                                       start=(i == 0 and gg == 0),
                                       stop=(i == 63 and gg == 3),
                                       skip_group_check=True)
                        if gg == 3:
                            ins.then_inc(svk, 1)

                def kv_blk(pair, blk):
                    sl = 8192 * (pair % 2)
                    i = 16 * pair + blk
                    if pair == 0 and blk % 4 == 0:
                        t.wait_ge(svC, blk // 4 + 1)
                        t.wait_ge(sgC, blk // 4 + 1)
                    if blk == 0:
                        if i >= 1:
                            vk_mms(i - 1)
                        t.wait_ge(swkv[pair % 2], 32 * (pair // 2 + 1))
                    bk = kv_rot[(2 * i) % 3]
                    bv = kv_rot[(2 * i + 1) % 3]
                    if 2 * i >= 3:
                        t.wait_ge(svev, i - 1)
                    for ct in range(NCT):
                        ins = t.matmul(bk[:, :], xn_slice(blk, ct),
                                       wk_buf[:, sl + 512 * ct: sl + 512 * ct + 512],
                                       start=(ct == 0), stop=(ct == NCT - 1))
                        if ct == NCT - 1:
                            ins.then_inc(skvmm, 1)
                    if blk >= 1:
                        vk_mms(i - 1)
                    if 2 * i + 1 >= 3:
                        t.wait_ge(skev, i)
                    for ct in range(NCT):
                        ins = t.matmul(bv[:, :], xn_slice(blk, ct),
                                       wv_buf[:, sl + 512 * ct: sl + 512 * ct + 512],
                                       start=(ct == 0), stop=(ct == NCT - 1))
                        if ct == NCT - 1:
                            ins.then_inc(skvmm, 1)

                t.wait_ge(scst, 16 * 8)
                for tg in range(NTAU):
                    t.wait_ge(sxd[tg % 2], 32 * (tg // 2) + 16)
                    if tg >= 1:
                        t.wait_ge(svCp, tg)
                    for ct in range(NCT):
                        ci = NCT * tg + ct
                        if ct == 8:
                            t.wait_ge(sxd[tg % 2], 32 * (tg // 2 + 1))
                        t.wait_ge(ssq, ci + 1)
                        t.matmul(ps_sx[0:1, :], onecol_bf[:, :], xbf(tg, ct),
                                 start=(ct == 0), stop=(ct == NCT - 1))
                        t.matmul(ps_sx2[0:1, :], onecol_bf[:, :], x2t(ci),
                                 start=(ct == 0), stop=(ct == NCT - 1)).then_inc(sstc, 1)
                    t.wait_ge(ssr, tg + 1)
                    t.wait_ge(svB, tg + 1)
                    t.wait_ge(svA, 2 * tg + 2)
                    t.matmul(ps_b1[:, :], onesrow_f[0:1, :], r_row[0:1, :],
                             start=True, stop=True)
                    t.matmul(ps_b2[:, :], onesrow_f[0:1, :], s0_row[0:1, :],
                             start=True, stop=True).then_inc(sbc, 1)
                    if tg == 1:
                        for blk2 in range(0, 8):
                            kv_blk(0, blk2)
                    elif tg == 2:
                        for blk2 in range(8, 12):
                            kv_blk(0, blk2)

                for blk in range(12, 16):
                    kv_blk(0, blk)
                for pair in range(1, NPAIR):
                    for blk in range(16):
                        kv_blk(pair, blk)
                vk_mms(63)

            @block.vector
            def _(v):
                def kv_vev(i):
                    v.wait_ge(skvmm, 2 * i + 2)
                    if i >= 2:
                        v.wait_ge(svk, i - 1)
                    v.tensor_copy(v_buf[:, 512 * (i % 2): 512 * (i % 2) + 512],
                                  kv_rot[(2 * i + 1) % 3][:, :]).then_inc(svev, 1)

                for tg in range(NTAU):
                    v.wait_ge(sstc, NCT * (tg + 1))
                    if tg >= 1:
                        v.wait_ge(ssr, tg)
                    v.tensor_scalar(m_row[:, :], ps_sx[0:1, :], 1.0 / C, None, ALU.mult)
                    v.tensor_scalar(ex2_row[:, :], ps_sx2[0:1, :], 1.0 / C, None,
                                    ALU.mult).then_inc(svA, 1)
                    v.tensor_tensor(msq_row[:, :], m_row[:, :], m_row[:, :], ALU.mult)
                    v.tensor_sub(var_row[:, :], ex2_row[:, :], msq_row[:, :]).then_inc(svA, 1)
                    v.wait_ge(ssr, tg + 1)
                    v.scalar_tensor_tensor(s0_row[:, :], m_row[:, :], -1.0, r_row[:, :],
                                           ALU.mult, ALU.mult).then_inc(svB, 1)
                    v.wait_ge(sbc, tg + 1)
                    if tg >= 1:
                        v.wait_ge(sgC, tg)
                    v.tensor_copy(s1b_sb[:, :], ps_b1[:, :])
                    v.tensor_copy(s0b_sb[:, :], ps_b2[:, :]).then_inc(svCp, 1)
                    for ct in range(10):
                        v.tensor_tensor(tmp_v[:, :], xbf(tg, ct), s1b_sb[:, :], ALU.mult)
                        ins = v.tensor_tensor(xn_dst_tile(tg, ct), tmp_v[:, :],
                                              s0b_sb[:, :], ALU.add)
                        if ct == 9:
                            ins.then_inc(svC, 1)
                    if tg == 1:
                        for i2 in range(0, 8):
                            kv_vev(i2)
                    elif tg == 2:
                        for i2 in range(8, 12):
                            kv_vev(i2)
                for i in range(12, 64):
                    v.wait_ge(skvmm, 2 * i + 2)
                    if i >= 2:
                        v.wait_ge(svk, i - 1)
                    v.tensor_copy(v_buf[:, 512 * (i % 2): 512 * (i % 2) + 512],
                                  kv_rot[(2 * i + 1) % 3][:, :]).then_inc(svev, 1)
                    if i in (18, 34, 50):
                        p2 = i // 16 - 1
                        v.wait_ge(svk, 16 * (p2 + 1))
                        v.tensor_copy(cc_in_sb[:, 512 * p2: 512 * p2 + 512],
                                      vk_ps[p2][:, :]).then_inc(sccv, 1)
                v.wait_ge(svk, 64)
                v.tensor_copy(cc_in_sb[:, 1536: 2048], vk_ps[3][:, :])
                v.tensor_copy(cc_in_sb[:, 2048: 2064], ksum_ps[:, 0:16]).then_inc(sccv, 1)

            @block.sync
            def _(sy):
                for p2 in range(4):
                    sy.wait_ge(sccv, p2 + 1)
                    lo = 512 * p2
                    hi = 2064 if p2 == 3 else 512 * p2 + 512
                    sy.dma_start(cc_in_d[:, lo: hi],
                                 cc_in_sb[:, lo: hi]).then_inc(sccd, 16)

        for g in reversed(g_wkv):
            g.__exit__(None, None, None)
        g_ks.__exit__(None, None, None)
        for g in reversed(g_vk):
            g.__exit__(None, None, None)

        g_b5pre = [nc.sbuf_tensor("wgu_buf", [128, 4 * 4096], BF16)]
        wgu_buf, = [g.__enter__() for g in g_b5pre]
        g_q = nc.sbuf_tensor("q_sb", [128, NCT * 2048], BF16)
        q_sb = g_q.__enter__()
        g_sS = [nc.sbuf_tensor("snum_sb", [128, 16 * 128], BF16),
                nc.sbuf_tensor("sden_sb", [128, 16 * 128], BF16),
                nc.sbuf_tensor("wp_buf", [128, 2 * 2048], BF16),
                nc.sbuf_tensor("xres_buf", [128, 2 * 2048], BF16),
                nc.sbuf_tensor("ks_sb", [128, 16], F32)]
        snum_sb, sden_sb, wp_buf, xres_buf, ks_sb = [g.__enter__() for g in g_sS]

        def wgu_issue(g, gi2):
            if gi2 >= 4:
                g.wait_ge(sgumm, 2 * (gi2 - 3))
            g.wait_ge(swgu[gi2 % 4], 16 * (gi2 // 4))
            g.dma_start(wgu_buf[:, 4096 * (gi2 % 4): 4096 * (gi2 % 4) + 4096],
                        wgu8[gi2 % NHT, :, :]).then_inc(swgu[gi2 % 4], 16)

        def wp_issue(g, dt):
            g.wait_ge(swp[dt % 2], 16 * (dt // 2))
            g.dma_start(wp_buf[:, 2048 * (dt % 2): 2048 * (dt % 2) + 2048],
                        wpT[dt, :, :]).then_inc(swp[dt % 2], 16)

        def xres_issue(g, dt):
            g.wait_ge(sxres[dt % 2], 16 * (dt // 2))
            g.dma_start(xres_buf[:, 2048 * (dt % 2): 2048 * (dt % 2) + 2048],
                        xT[dt, :, 0:2048]).then_inc(sxres[dt % 2], 16)

        with contextlib.ExitStack() as _st3:
            q_rot0 = _st3.enter_context(nc.psum_tensor("q_rot0", [128, 512], F32))
            q_rot1 = _st3.enter_context(nc.psum_tensor("q_rot1", [128, 512], F32))
            q_rot2 = _st3.enter_context(nc.psum_tensor("q_rot2", [128, 512], F32))
            nd_num0 = _st3.enter_context(nc.psum_tensor("nd_num0", [128, 512], F32))
            nd_num1 = _st3.enter_context(nc.psum_tensor("nd_num1", [128, 512], F32))
            nd_den0 = _st3.enter_context(nc.psum_tensor("nd_den0", [128, 512], F32))
            nd_den1 = _st3.enter_context(nc.psum_tensor("nd_den1", [128, 512], F32))
            rec_st = _st3.enter_context(nc.sbuf_tensor("rec_st", [128, 3 * 512], F32))
            out1_st = _st3.enter_context(nc.sbuf_tensor("out1_st", [128, 4 * 512], F32))
            block = _st3.enter_context(nc.Block("b34", no_gpsimd_drain=True))
            q_rot = [q_rot0, q_rot1, q_rot2]
            nd_num = [nd_num0, nd_num1]
            nd_den = [nd_den0, nd_den1]
            pj_rot = [q_rot0, q_rot1]

            @block.sync
            def _(sy):
                sy.wait_ge(scc, 1)
                sy.dma_start(ks_sb[:, :], cc_out_d[:, 2048: 2064]).then_inc(sccb, 16)
                for idx in range(64):
                    dt, tau = idx // 4, idx % 4
                    sy.wait_ge(so1ev, idx + 1)
                    sy.wait_ge(so1st[idx % 2], 16 * (idx // 2))
                    sy.dma_start(out_d[dt, :, 512 * tau: 512 * tau + 512],
                                 out1_st[:, 512 * (idx % 4): 512 * (idx % 4) + 512]
                                 ).then_inc(so1st[idx % 2], 16)


            @block.gpsimd
            def _(g):
                for ot in range(2, 16):
                    g.wait_ge(sqmm, 4 * (ot - 1))
                    wq_issue(g, ot)
                    if ot == 3:
                        g.wait_ge(scc, 1)
                        g.dma_start(snum_sb[:, :], cc_out_d[:, 0: 2048]).then_inc(sccb, 16)
                wp_issue(g, 0)
                xres_issue(g, 0)
                wp_issue(g, 1)
                xres_issue(g, 1)
                for dt in range(2, 16):
                    g.wait_ge(spmm, 4 * (dt - 1))
                    wp_issue(g, dt)
                    g.wait_ge(so1ev, 4 * (dt - 1))
                    xres_issue(g, dt)
                for gi2 in range(4):
                    wgu_issue(g, gi2)

            @block.tensor
            def _(t):
                def nd_pair(j):
                    gq, tau = j // 4, j % 4
                    qs = q_sb[:, 2048 * gq + 512 * tau: 2048 * gq + 512 * tau + 512]
                    t.wait_ge(sqev, j + 1)
                    t.wait_ge(sSb, gq + 1)
                    if j >= 2:
                        t.wait_ge(ssrec, 2 * j - 3)
                    t.matmul(nd_den[j % 2][:, :], sden_sb[:, 128 * gq: 128 * gq + 128],
                             qs, start=True, stop=True).then_inc(snd, 1)
                    if j >= 2:
                        t.wait_ge(svattn, j - 1)
                    t.matmul(nd_num[j % 2][:, :], snum_sb[:, 128 * gq: 128 * gq + 128],
                             qs, start=True, stop=True).then_inc(snd, 1)

                for kind, i in B34_SCHED:
                    if kind == "nd":
                        nd_pair(i)
                        continue
                    ot, tau = i // 4, i % 4
                    if tau == 0:
                        t.wait_ge(swq[ot % 2], 16 * (ot // 2 + 1))
                    bank = q_rot[i % 3]
                    if i >= 3:
                        t.wait_ge(sqev, i - 2)
                    for ct in range(NCT):
                        ins = t.matmul(
                            bank[:, :],
                            wq_buf[:, 2048 * (ot % 2) + 128 * ct: 2048 * (ot % 2) + 128 * ct + 128],
                            xn_loc[:, 2048 * ct + 512 * tau: 2048 * ct + 512 * tau + 512],
                            start=(ct == 0), stop=(ct == NCT - 1))
                        if ct == NCT - 1:
                            ins.then_inc(sqmm, 1)
                t.wait_ge(sqev, 64)
                for dt in range(16):
                    t.wait_ge(swp[dt % 2], 16 * (dt // 2 + 1))
                    for tau in range(NTAU):
                        idx = 4 * dt + tau
                        t.wait_ge(svattn, 61 + tau)
                        if idx >= 2:
                            t.wait_ge(so1ev, idx - 1)
                        bank = pj_rot[idx % 2]
                        for ct in range(NCT):
                            ins = t.matmul(
                                bank[:, :],
                                wp_buf[:, 2048 * (dt % 2) + 128 * ct: 2048 * (dt % 2) + 128 * ct + 128],
                                q_sb[:, 2048 * ct + 512 * tau: 2048 * ct + 512 * tau + 512],
                                start=(ct == 0), stop=(ct == NCT - 1))
                            if ct == NCT - 1:
                                ins.then_inc(spmm, 1)

            @block.scalar
            def _(s):
                def relu(k):
                    ot, tau = k // 4, k % 4
                    s.wait_ge(sqmm, k + 1)
                    s.activation(q_sb[:, 2048 * ot + 512 * tau: 2048 * ot + 512 * tau + 512],
                                 q_rot[k % 3][:, :], AF.Relu).then_inc(sqev, 1)

                def rec(j):
                    sl = rec_st[:, 512 * (j % 3): 512 * (j % 3) + 512]
                    s.wait_ge(snd, 2 * j + 1)
                    if j >= 3:
                        s.wait_ge(svattn, j - 2)
                    s.activation(sl, nd_den[j % 2][:, :], AF.Ln,
                                 bias=epsb[:, 1:2]).then_inc(ssrec, 1)
                    s.activation(sl, sl, AF.Exp, scale=-1.0).then_inc(ssrec, 1)

                for kind, i in B34_SCHED:
                    if kind == "q":
                        relu(i)
                    else:
                        rec(i)

            @block.vector
            def _(v):
                v.wait_ge(sccb, 32)
                for gq in range(16):
                    v.tensor_tensor(snum_sb[:, 128 * gq: 128 * gq + 128],
                                    snum_sb[:, 128 * gq: 128 * gq + 128],
                                    mask_bf[:, :], ALU.mult)
                    v.tensor_scalar(sden_sb[:, 128 * gq: 128 * gq + 128],
                                    mask_bf[:, :], ks_sb[:, gq: gq + 1],
                                    None, ALU.mult).then_inc(sSb, 1)
                for j in range(64):
                    gq, tau = j // 4, j % 4
                    v.wait_ge(snd, 2 * j + 2)
                    v.wait_ge(ssrec, 2 * j + 2)
                    v.tensor_tensor(
                        q_sb[:, 2048 * gq + 512 * tau: 2048 * gq + 512 * tau + 512],
                        nd_num[j % 2][:, :],
                        rec_st[:, 512 * (j % 3): 512 * (j % 3) + 512],
                        ALU.mult).then_inc(svattn, 1)
                for idx in range(64):
                    dt, tau = idx // 4, idx % 4
                    v.wait_ge(spmm, idx + 1)
                    v.wait_ge(sxres[dt % 2], 16 * (dt // 2 + 1))
                    if idx >= 4:
                        v.wait_ge(so1st[idx % 2], 16 * (idx // 2 - 1))
                    v.scalar_tensor_tensor(
                        out1_st[:, 512 * (idx % 4): 512 * (idx % 4) + 512],
                        pj_rot[idx % 2][:, :],
                        bproj_sb[:, dt: dt + 1],
                        xres_buf[:, 2048 * (dt % 2) + 512 * tau: 2048 * (dt % 2) + 512 * tau + 512],
                        ALU.add, ALU.add).then_inc(so1ev, 1)

        g_wq.__exit__(None, None, None)
        for g in reversed(g_sS):
            g.__exit__(None, None, None)
        g_q.__exit__(None, None, None)

        with contextlib.ExitStack() as _st5:
            hid_sb = _st5.enter_context(nc.sbuf_tensor("hid_sb", [128, (NHT + 3) * 512], BF16))

            def hsl(tau, ht):
                return 512 * ((NHT * tau + ht) % (NHT + 3))
            gt_st = _st5.enter_context(nc.sbuf_tensor("gt_st", [128, 3 * 512], BF16))
            wd_buf = _st5.enter_context(nc.sbuf_tensor("wd_buf", [128, 2 * HID], BF16))
            outf_st = _st5.enter_context(nc.sbuf_tensor("outf_st", [128, 2 * 512], F32))
            g_rot0 = _st5.enter_context(nc.psum_tensor("g_rot0", [128, 512], F32))
            g_rot1 = _st5.enter_context(nc.psum_tensor("g_rot1", [128, 512], F32))
            g_rot2 = _st5.enter_context(nc.psum_tensor("g_rot2", [128, 512], F32))
            u_rot0 = _st5.enter_context(nc.psum_tensor("u_rot0", [128, 512], F32))
            u_rot1 = _st5.enter_context(nc.psum_tensor("u_rot1", [128, 512], F32))
            u_rot2 = _st5.enter_context(nc.psum_tensor("u_rot2", [128, 512], F32))
            d_rot0 = _st5.enter_context(nc.psum_tensor("d_rot0", [128, 512], F32))
            d_rot1 = _st5.enter_context(nc.psum_tensor("d_rot1", [128, 512], F32))
            block = _st5.enter_context(nc.Block("b5", no_gpsimd_drain=True))
            g_rot = [g_rot0, g_rot1, g_rot2]
            u_rot = [u_rot0, u_rot1, u_rot2]
            d_rot = [d_rot0, d_rot1]

            @block.gpsimd
            def _(g):
                def wgu_dma(gi2):
                    wgu_issue(g, gi2)

                def wd_dma(di2):
                    if di2 >= 2:
                        g.wait_ge(sdmm, di2 - 1)
                    g.wait_ge(swd[di2 % 2], 16 * (di2 // 2))
                    g.dma_start(wd_buf[:, HID * (di2 % 2): HID * (di2 % 2) + HID],
                                wd8[di2 % 16, :, :]).then_inc(swd[di2 % 2], 16)

                for tau in range(NTAU):
                    for ht in range(NHT):
                        gi = NHT * tau + ht
                        if 4 <= gi + 2 < NHT * NTAU:
                            wgu_dma(gi + 2)
                    wd_dma(16 * tau)
                    wd_dma(16 * tau + 1)
                    for dt in range(16):
                        di = 16 * tau + dt
                        if dt + 2 < 16:
                            wd_dma(di + 2)
                        if di == 0:
                            g.wait_ge(so1st[0], 16 * 32)
                            g.wait_ge(so1st[1], 16 * 32)
                        g.wait_ge(sfev, di + 1)
                        g.wait_ge(sfst[di % 2], 16 * (di // 2))
                        g.dma_start(out_d[dt, :, 512 * tau: 512 * tau + 512],
                                    outf_st[:, 512 * (di % 2): 512 * (di % 2) + 512],
                                    accum_op=ALU.add).then_inc(sfst[di % 2], 16)

            @block.tensor
            def _(t):
                for tau in range(NTAU):
                    for ht in range(NHT):
                        gi = NHT * tau + ht
                        t.wait_ge(ssubg, gi + 1)
                        gb, ub = g_rot[gi % 3], u_rot[gi % 3]
                        if gi >= 3:
                            t.wait_ge(sgev, gi - 2)
                        for ct in range(NCT):
                            ins = t.matmul(
                                gb[:, :],
                                wgu_buf[:, 4096 * (gi % 4) + 128 * ct: 4096 * (gi % 4) + 128 * ct + 128],
                                xn_loc[:, 2048 * ct + 512 * tau: 2048 * ct + 512 * tau + 512],
                                start=(ct == 0), stop=(ct == NCT - 1))
                            if ct == NCT - 1:
                                ins.then_inc(sgumm, 1)
                        if gi >= 3:
                            t.wait_ge(shid, gi - 2)
                        for ct in range(NCT):
                            ins = t.matmul(
                                ub[:, :],
                                wgu_buf[:, 4096 * (gi % 4) + 2048 + 128 * ct: 4096 * (gi % 4) + 2048 + 128 * ct + 128],
                                xn_loc[:, 2048 * ct + 512 * tau: 2048 * ct + 512 * tau + 512],
                                start=(ct == 0), stop=(ct == NCT - 1))
                            if ct == NCT - 1:
                                ins.then_inc(sgumm, 1)
                    for dt in range(16):
                        di = 16 * tau + dt
                        t.wait_ge(ssubd, di + 1)
                        t.wait_ge(shid, NHT * (tau + 1) - (1 if dt == 0 else 0))
                        db = d_rot[di % 2]
                        if di >= 2:
                            t.wait_ge(sfev, di - 1)
                        for ht in range(NHT):
                            if dt == 0 and ht == NHT - 1:
                                t.wait_ge(shid, NHT * (tau + 1))
                            ins = t.matmul(
                                db[:, :],
                                wd_buf[:, HID * (di % 2) + 128 * ht: HID * (di % 2) + 128 * ht + 128],
                                hid_sb[:, hsl(tau, ht): hsl(tau, ht) + 512],
                                start=(ht == 0), stop=(ht == NHT - 1))
                            if ht == NHT - 1:
                                ins.then_inc(sdmm, 1)

            @block.scalar
            def _(s):
                for tau in range(NTAU):
                    for ht in range(NHT):
                        gi = NHT * tau + ht
                        s.wait_ge(sgumm, 2 * gi + 1)
                        if gi >= 3:
                            s.wait_ge(shid, gi - 2)
                        s.activation(gt_st[:, 512 * (gi % 3): 512 * (gi % 3) + 512],
                                     g_rot[gi % 3][:, :], AF.Silu,
                                     scale=sg_sb[:, ht: ht + 1]).then_inc(sgev, 1)

            @block.vector
            def _(v):
                def sub_g(gi2):
                    v.wait_ge(swgu[gi2 % 4], 16 * (gi2 // 4 + 1))
                    v.tensor_scalar(wgu_buf[:, 4096 * (gi2 % 4): 4096 * (gi2 % 4) + 4096],
                                    wgu_buf[:, 4096 * (gi2 % 4): 4096 * (gi2 % 4) + 4096],
                                    -128.0, None, ALU.add).then_inc(ssubg, 1)

                def sub_d(di2):
                    v.wait_ge(swd[di2 % 2], 16 * (di2 // 2 + 1))
                    v.tensor_scalar(wd_buf[:, HID * (di2 % 2): HID * (di2 % 2) + HID],
                                    wd_buf[:, HID * (di2 % 2): HID * (di2 % 2) + HID],
                                    -128.0, None, ALU.add).then_inc(ssubd, 1)

                sub_g(0)
                sub_g(1)
                for tau in range(NTAU):
                    for ht in range(NHT):
                        gi = NHT * tau + ht
                        if 1 <= ht <= NHT - 2:
                            sub_g(gi + 1)
                        v.wait_ge(sgumm, 2 * gi + 2)
                        v.wait_ge(sgev, gi + 1)
                        if ht == 3 and tau >= 1:
                            v.wait_ge(sdmm, 16 * tau)
                        v.scalar_tensor_tensor(hid_sb[:, hsl(tau, ht): hsl(tau, ht) + 512],
                                               u_rot[gi % 3][:, :], su_sb[:, ht: ht + 1],
                                               gt_st[:, 512 * (gi % 3): 512 * (gi % 3) + 512],
                                               ALU.mult, ALU.mult).then_inc(shid, 1)
                    sub_d(16 * tau)
                    for dt in range(16):
                        di = 16 * tau + dt
                        if dt < 15:
                            sub_d(di + 1)
                        if tau + 1 < NTAU and dt in (1, 2):
                            sub_g(NHT * (tau + 1) + dt - 1)
                        v.wait_ge(sdmm, di + 1)
                        if di >= 2:
                            v.wait_ge(sfst[di % 2], 16 * (di // 2))
                        v.tensor_scalar(outf_st[:, 512 * (di % 2): 512 * (di % 2) + 512],
                                        d_rot[di % 2][:, :], sd_sb[:, dt: dt + 1],
                                        None, ALU.mult).then_inc(sfev, 1)

            @block.sync
            def _(sy):
                sy.wait_ge(sfst[0], 16 * 32)
                sy.wait_ge(sfst[1], 16 * 32)

        for g in reversed(g_b5pre):
            g.__exit__(None, None, None)

    return nc


_PROG_CACHE = {}


def _get_program():
    if "nc" not in _PROG_CACHE:
        _PROG_CACHE["nc"] = build_program()
    return _PROG_CACHE["nc"]


def _u8(a_int32):
    a = np.ascontiguousarray(a_int32)
    return a.view(np.uint8).reshape(a.shape + (4,))[..., 0]


def kernel(L0_attn_x, L0_attn_w_qkv, L0_attn_w_proj, L0_attn_b_proj,
           L0_ffn_wq_gate, L0_ffn_s_gate, L0_ffn_wq_up, L0_ffn_s_up,
           L0_ffn_wq_down, L0_ffn_s_down):
    x = np.asarray(L0_attn_x, dtype=np.float32)
    w_qkv = np.asarray(L0_attn_w_qkv, dtype=np.float32)
    w_proj = np.asarray(L0_attn_w_proj, dtype=np.float32)
    b_proj = np.asarray(L0_attn_b_proj, dtype=np.float32)
    wq_gate = _u8(np.asarray(L0_ffn_wq_gate))
    s_gate = np.asarray(L0_ffn_s_gate, dtype=np.float32)
    wq_up = _u8(np.asarray(L0_ffn_wq_up))
    s_up = np.asarray(L0_ffn_s_up, dtype=np.float32)
    wq_down = _u8(np.asarray(L0_ffn_wq_down))
    s_down = np.asarray(L0_ffn_s_down, dtype=np.float32)

    B, N, Cm = x.shape
    assert (B, N, Cm) == (4, 4096, 2048)

    def tile_oT(w, n_out_t, n_in_t, fo):
        a = np.ascontiguousarray(w.T)
        a = a.reshape(n_in_t, 128, n_out_t, fo).transpose(2, 1, 0, 3)
        return np.ascontiguousarray(a.reshape(n_out_t, 128, n_in_t * fo))

    wqTt = tile_oT(w_qkv[0:2048], 16, 16, 128).astype(np.float32)
    wkvTt = tile_oT(w_qkv[2048:6144], 8, 16, 512).astype(np.float32)
    wpTt = tile_oT(w_proj, 16, 16, 128).astype(np.float32)
    wgt = tile_oT(wq_gate, 44, 16, 128)
    wut = tile_oT(wq_up, 44, 16, 128)
    wgu8 = np.ascontiguousarray(np.concatenate([wgt, wut], axis=2))
    wd8 = np.ascontiguousarray(tile_oT(wq_down, 16, 44, 128))

    shared = {
        "wqT": wqTt, "wkvT": wkvTt, "wpT": wpTt, "wgu8": wgu8, "wd8": wd8,
        "bprojT": np.ascontiguousarray(b_proj.reshape(16, 128).T),
        "sgT": np.ascontiguousarray(s_gate.reshape(44, 128).T),
        "suT": np.ascontiguousarray(s_up.reshape(44, 128).T),
        "sdT": np.ascontiguousarray(s_down.reshape(16, 128).T),
        "onesc": np.ones((128, 1), dtype=np.float32),
        "onesr": np.ones((1, 128), dtype=np.float32),
        "maskc": np.kron(np.eye(4, dtype=np.float32),
                         np.ones((32, 32), dtype=np.float32)),
        "epsc": np.ascontiguousarray(
            np.broadcast_to(np.array([1e-5, 1e-15], dtype=np.float32), (128, 2))),
    }

    in_maps = []
    for core in range(8):
        b, h = core // 2, core % 2
        T = x[b].T
        xTc = np.ascontiguousarray(
            T[:, 2048 * h: 2048 * h + 2048].reshape(16, 128, 2048))
        m = {"xT": xTc}
        m.update(shared)
        in_maps.append(m)

    nc = _get_program()
    res = run_bass_kernel_spmd(nc, in_maps, core_ids=list(range(8)))

    out = np.empty((4, 4096, 2048), dtype=np.float32)
    for core in range(8):
        b, h = core // 2, core % 2
        oc = np.asarray(res.results[core]["out"])
        out[b, 2048 * h: 2048 * h + 2048, :] = oc.transpose(2, 0, 1).reshape(2048, 2048)
    return out
